# version 01
# baseline (speedup 1.0000x reference)
import numpy as np
import concourse.bass as bass
import concourse.mybir as mybir
from concourse.bass_utils import run_bass_kernel_spmd
from concourse.library_config import mlp as _mlp_lib

F32 = mybir.dt.float32
BF16 = mybir.dt.bfloat16
I16 = mybir.dt.int16
AF = mybir.ActivationFunctionType
ALU = mybir.AluOpType
AX = mybir.AxisListType

B, TFULL, D = 4, 4096, 1024
TOWN = 2048
E, CAP, HDIM = 8, 640, 4096
NCHH = 16
DC = 8
NC_N = 32
NC_H = 32
DUMP = E * CAP
DISP_ROWS = 5248
TT = 256
NTT = TOWN // TT
EPS = 1e-5


class Sem:
    def __init__(self, nc, name):
        self.h = nc.semaphore(name).__enter__()
        self.n = 0

    def inc(self, instr, k=1):
        instr.then_inc(self.h, k)
        self.n += k


def build_program(upto=5, attn_sub=3, debug=False):
    nc = bass.Bass("TRN2", target_bir_lowering=False,
                   detect_race_conditions=False)

    x_own = nc.dram_tensor("x_own", [TOWN, D], F32, kind="ExternalInput")
    x_oth = nc.dram_tensor("x_oth", [TOWN, D], F32, kind="ExternalInput")
    hmask_in = nc.dram_tensor("hmask", [128, 1], F32, kind="ExternalInput")
    kT_in = nc.dram_tensor("kT", [D, 4096], F32, kind="ExternalInput")
    v_in = nc.dram_tensor("v", [4096, D], F32, kind="ExternalInput")
    wr_in = nc.dram_tensor("wr", [128, DC, E], F32, kind="ExternalInput")
    w1_in = nc.dram_tensor("w1", [E, D, HDIM], F32, kind="ExternalInput")
    b1_in = nc.dram_tensor("b1", [128, E, NC_H], F32, kind="ExternalInput")
    w2_in = nc.dram_tensor("w2", [E, HDIM, D], F32, kind="ExternalInput")
    b2_in = nc.dram_tensor("b2", [E, D], F32, kind="ExternalInput")
    out_t = nc.dram_tensor("out", [TOWN, D], F32, kind="ExternalOutput")

    if debug:
        dbg_logits = nc.dram_tensor("dbg_logits", [128, 2, NCHH, E], F32,
                                    kind="ExternalOutput")
        dbg_wgt = nc.dram_tensor("dbg_wgt", [128, 2, NCHH], F32,
                                 kind="ExternalOutput")
        dbg_idx = nc.dram_tensor("dbg_idx", [128, 6, 128], I16,
                                 kind="ExternalOutput")
        dbg_disp = nc.dram_tensor("dbg_disp", [DISP_ROWS, D], F32,
                                  kind="ExternalOutput")
        dbg_s = nc.dram_tensor("dbg_s", [TOWN, D], F32, kind="ExternalOutput")
        dbg_y = nc.dram_tensor("dbg_y", [E * CAP, D], F32,
                               kind="ExternalOutput")
        dbg_g = nc.dram_tensor("dbg_g", [128, 2, NCHH, D], F32,
                               kind="ExternalOutput")
        dbg_rt = nc.dram_tensor("dbg_rt", [128, NCHH, 10], F32,
                                kind="ExternalOutput")
        dbg_p = nc.dram_tensor("dbg_p", [128, 2, NCHH], F32,
                               kind="ExternalOutput")
        dbg_eouts = nc.dram_tensor("dbg_eouts", [128, 2, 2, NCHH], F32,
                                   kind="ExternalOutput")
        dbg_eab = nc.dram_tensor("dbg_eab", [16, 2, 256], F32,
                                 kind="ExternalOutput")
        dbg_pos = nc.dram_tensor("dbg_pos", [16, 2, 256], F32,
                                 kind="ExternalOutput")
        dbg_p2s = nc.dram_tensor("dbg_p2s", [16, 8, 256], F32,
                                 kind="ExternalOutput")
        dbg_off = nc.dram_tensor("dbg_off", [32, 3, 8], F32,
                                 kind="ExternalOutput")
        dbg_eT = nc.dram_tensor("dbg_eT", [16, 2, 256], F32,
                                kind="ExternalOutput")
        dbg_hl = nc.dram_tensor("dbg_hl", [128, 2, DC, 512], F32,
                                kind="ExternalOutput")
    disp = nc.dram_tensor("disp", [DISP_ROWS, D], BF16)
    y_hbm = nc.dram_tensor("y_hbm", [E * CAP, D], BF16)
    s_hbm = nc.dram_tensor("s_hbm", [TOWN, D], BF16)

    base0 = nc.sbuf_base
    cap_bytes = nc.SBUF_PARTITION_SIZE_BYTES - base0

    def at(off, name, shape, dtype):
        nc.sbuf_base = base0 + off
        t = nc.alloc_sbuf_tensor(name, shape, dtype)
        used = nc.sbuf_base - base0
        assert used <= cap_bytes, (name, used, cap_bytes)
        return t

    _po = [0]

    def persist(name, shape, dtype):
        nc.sbuf_base = base0 + _po[0]
        t = nc.alloc_sbuf_tensor(name, shape, dtype)
        _po[0] = ((nc.sbuf_base - base0) + 31) & ~31
        assert _po[0] <= 14176, (name, _po[0])
        return t

    ident_f = persist("ident_f", [128, 128], F32)
    ident_b = persist("ident_b", [128, 128], BF16)
    iod = persist("iod", [128, 128], BF16)
    ones_n = persist("ones_n", [128, 1], BF16)
    ones_1 = persist("ones_1", [1, 128], BF16)
    ones_nn = persist("ones_nn", [128, 128], BF16)
    tril16 = persist("tril16", [16, 16], F32)
    m32b = persist("m32b", [32, 16], F32)
    iot32 = persist("iot32", [32, 16], F32)
    tot_cat = persist("tot_cat", [32, 8], F32)
    ones16 = persist("ones16", [16, 1], F32)
    ones1x16 = persist("ones1x16", [1, 16], F32)
    iot16 = persist("iot16", [16, 16], F32)
    hmask = persist("hmaskb", [128, 1], F32)
    hinv = persist("hinv", [128, 1], F32)
    wr_sb = persist("wr_sb", [128, DC, E], F32)
    wr_hi = persist("wr_hi", [128, DC, E], BF16)
    wr_lo = persist("wr_lo", [128, DC, E], BF16)
    wtmp = persist("wtmp", [16, 2, 128], F32)
    rt3 = persist("rt3", [128, 8], F32)
    b1_sb = persist("b1_sb", [128, E, NC_H], F32)
    b2b_sb = persist("b2b_sb", [128, D], BF16)
    idx_all = persist("idx_all", [128, 6, 128], I16)
    logits = persist("logits", [128, 2, NCHH, E], F32)
    p_own = persist("p_own", [128, 2, NCHH], F32)
    wgt = persist("wgt", [128, 2, NCHH], F32)
    iota8 = persist("iota8", [128, 1, E], F32)
    iota8m = persist("iota8m", [128, 1, E], F32)
    scrA = persist("scrA", [128, NCHH, E], F32)
    scrB = persist("scrB", [128, NCHH, E], F32)
    eouts = persist("eouts", [128, 2, 2, NCHH], F32)
    rt_s = persist("rt_s", [128, NCHH, 10], F32)
    c_f = persist("c_f", [128, TT], F32)
    c_b = persist("c_b", [128, 1, TT], BF16)

    R1, R2, R3, R4 = 14176, 46944, 112480, 178016
    xn_fm = at(R1, "xn_fm", [128, DC, TOWN], BF16)
    h_lo = at(R1, "h_lo", [128, 24, CAP], BF16)
    s_sb = at(R1, "s_sb", [128, NCHH, D], BF16)
    xn_tm = [at(R2, "xn_tm_o", [128, NCHH, D], BF16),
             at(R2 + 32768, "xn_tm_x", [128, NCHH, D], BF16)]
    v_sb = at(R2, "v_sb", [128, NC_N, D], BF16)
    w1_sb = at(R2, "w1_sb", [128, DC, HDIM], BF16)
    gA = at(R2, "gA", [128, NCHH, D], BF16)
    gB = at(R2 + 32768, "gB", [128, NCHH, D], BF16)
    x_tile = [at(R3, "x_tile0", [128, 4, D], F32),
              at(R3 + 16384, "x_tile1", [128, 4, D], F32)]
    xsq_t = at(R3 + 32768, "xsq_t", [128, 4, D], F32)
    xnf_t = at(R3 + 49152, "xnf_t", [128, 4, D], F32)
    kT_sb = at(R3, "kT_sb", [128, DC, 4096], BF16)
    w2_sb = at(R3, "w2_sb", [128, NC_H, D], BF16)
    x_own_sb = at(R3, "x_own_sb", [128, NCHH, D], F32)
    xnf_fm = at(R4, "xnf_fm", [128, DC, 512], F32)
    z_sb = at(R4 + 16384, "z_sb", [128, 8, D], BF16)
    xnf_hi = at(R4 + 16384, "xnf_hi", [128, DC, 512], BF16)
    xnf_lo = at(R4 + 24576, "xnf_lo", [128, DC, 512], BF16)
    eT = at(R4, "eT", [16, 2, 256], F32)
    msk8 = at(R4 + 2048, "msk8", [16, E, 256], F32)
    incl8 = at(R4 + 10240, "incl8", [16, E, 256], F32)
    posAB = at(R4 + 18432, "posAB", [16, 2, 256], F32)
    eAB = at(R4 + 20480, "eAB", [16, 2, 256], F32)
    p2s = at(R4 + 22528, "p2s", [16, 8, 256], F32)
    totG = at(R4 + 30720, "totG", [16, E, 1], F32)
    offG = at(R4 + 31008, "offG", [16, 2, E], F32)
    sumA = at(R4 + 31520, "sumA", [1, E], F32)
    aw_sb = at(R4, "aw_sb", [128, NC_N, TT], BF16)
    sq_sb = at(R4 + 16384, "sq_sb", [128, 2, TT], BF16)
    sqacc = at(R4 + 25600, "sqacc", [128, TT], F32)
    sqacc_b = at(R4 + 26624, "sqacc_b", [128, TT], BF16)
    s_tile = [at(R4 + 17408, "s_tile0", [128, 2, D], BF16),
              at(R4 + 21504, "s_tile1", [128, 2, D], BF16)]
    h_hi = at(R4, "h_hi", [128, 8, CAP], BF16)
    buf_tm = at(R4 + 10240, "buf_tm", [128, 5, D], BF16)
    buf_fm = at(R4 + 20480, "buf_fm", [128, DC, CAP], BF16)
    y_tile = [at(R4 + 30720, "y_tile0", [128, D], BF16),
              at(R4 + 32768, "y_tile1", [128, D], BF16)]
    o_tile = [at(R4, "o_tile0", [128, D], F32),
              at(R4 + 4096, "o_tile1", [128, D], F32)]
    f_tile = at(R4 + 8192, "f_tile", [128, D], F32)
    g_tile = at(R4 + 12288, "g_tile", [128, D], F32)

    banks = []
    banks_b = []
    for k in range(8):
        nc.psum_base = k
        banks.append(nc.alloc_psum_tensor(f"bk{k}", [128, 512], F32))
    for k in range(2):
        nc.psum_base = k
        banks_b.append(nc.alloc_psum_tensor(f"bkb{k}", [128, 1024], BF16))

    sems = {}

    def S(name):
        if name not in sems:
            sems[name] = Sem(nc, name)
        return sems[name]

    def vfence(v):
        sf = S("vf")
        sf.inc(v.engine_nop())
        v.wait_ge(sf.h, sf.n)

    with nc.Block() as blk:
        @blk.gpsimd
        def _(g):
            g.iota(iod[:, :], [[-1, 128]], base=0, channel_multiplier=1,
                   allow_small_or_imprecise_dtypes=True)
            g.iota(iot16[:, :], [[-1, 16]], base=0, channel_multiplier=1,
                   allow_small_or_imprecise_dtypes=True)
            g.iota(iot32[:, :], [[-1, 16]], base=-16, channel_multiplier=1,
                   allow_small_or_imprecise_dtypes=True)
            g.iota(iota8[:, :, :], [[0, 1], [1, E]], base=0,
                   channel_multiplier=0, allow_small_or_imprecise_dtypes=True)
            S("iot").inc(g.iota(iota8m[:, :, :], [[0, 1], [1, E]], base=-999,
                   channel_multiplier=0,
                   allow_small_or_imprecise_dtypes=True))
            g.load_library(_mlp_lib)

        @blk.sync
        def _(sy):
            sm = S("small")
            sm.inc(sy.dma_start(hmask[:, :], hmask_in.ap()[:, :]), 16)
            sm.inc(sy.dma_start(wr_sb[:, :, :], wr_in.ap()[:, :, :]), 16)
            sm.inc(sy.dma_start(b1_sb[:, :, :], b1_in.ap()[:, :, :]), 16)

        @blk.vector
        def _(v):
            v.wait_ge(S("iot").h, 1)
            v.memset(scrA[:, :, :], 0.0)
            v.memset(rt3[:, :], 0.0)
            v.memset(idx_all[:, :, :], 0)
            v.memset(ones_n[:, :], 1.0)
            v.memset(ones_1[:, :], 1.0)
            v.memset(ones_nn[:, :], 1.0)
            v.memset(ones16[:, :], 1.0)
            v.memset(ones1x16[:, :], 1.0)
            v.memset(z_sb[:, :, :], 0.0)
            v.tensor_single_scalar(ident_b[:, :], iod[:, :], 0.0, ALU.is_equal)
            v.tensor_copy(ident_f[:, :], ident_b[:, :])
            v.tensor_single_scalar(tril16[:, :], iot16[:, :], 0.0, ALU.is_lt)
            v.tensor_single_scalar(m32b[:, :], iot32[:, :], 0.0, ALU.is_lt)
            v.wait_ge(S("small").h, 16)
            v.tensor_scalar(hinv[:, :], hmask[:, :], -1.0, 1.0,
                            ALU.mult, ALU.add)
            v.wait_ge(S("small").h, 48)
            v.tensor_copy(wr_hi[:, :, :], wr_sb[:, :, :])
            vfence(v)
            v.tensor_tensor(wr_lo[:, :, :], wr_sb[:, :, :], wr_hi[:, :, :],
                            ALU.subtract)
            S("const").inc(v.engine_nop())

        @blk.sync
        def _(sy):
            sy.wait_ge(S("const").h, 1)
            dap = disp.ap().rearrange("(g p) d -> p g d", p=128)
            for i in range(0, 41, 8):
                n = min(8, 41 - i)
                S("dzero").inc(sy.dma_start(dap[:, i:i + n, :],
                                            z_sb[:, 0:n, :]), 16)

    x_in = [x_own, x_oth]
    if upto < 1:
        mybir.codegen_inst_isa_subclasses(nc)
        return nc
    with nc.Block() as blk:
        @blk.sync
        def _(sy):
            for g in range(8):
                H, tl = g // 4, g % 4
                if g >= 2:
                    sy.wait_ge(S("xfree").h, g - 1)
                src = x_in[H].ap().rearrange("(c p) d -> p c d", p=128)
                S("xld").inc(sy.dma_start(
                    x_tile[g % 2][:, :, :], src[:, tl * 4:tl * 4 + 4, :]), 16)

        @blk.scalar
        def _(sc):
            for g in range(8):
                sc.wait_ge(S("xld").h, 16 * (g + 1))
                if g >= 1:
                    sc.wait_ge(S("xred").h, g)
                sc.activation(xsq_t[:, :, :], x_tile[g % 2][:, :, :],
                              AF.Square)
                S("xsq").inc(sc.copy(scrA[0:1, 0, 0:1], scrA[0:1, 0, 1:2]))
                sc.wait_ge(S("xst").h, g + 1)
                sc.activation(rt_s[:, 0:4, 1:2], rt_s[:, 0:4, 0:1], AF.Sqrt)
                S("xsqt").inc(sc.copy(scrA[0:1, 0, 0:1], scrA[0:1, 0, 1:2]))

        @blk.tensor
        def _(t):
            for g in range(8):
                t.wait_ge(S("xn").h, g + 1)
                for dc in range(DC):
                    tg = g * 8 + dc
                    if tg >= 2:
                        t.wait_ge(S("xtre").h, tg - 1)
                    bk = banks[tg % 2]
                    for i in range(4):
                        ins = t.transpose(bk[:, i * 128:(i + 1) * 128],
                                          xnf_t[:, i, dc * 128:(dc + 1) * 128],
                                          ident_f[:, :])
                    S("xtr").inc(ins)
                t.wait_ge(S("xtre").h, g * 8 + 8)
                t.wait_ge(S("small").h, 48)
                for i in range(4):
                    ri = g * 4 + i
                    if ri >= 4:
                        t.wait_ge(S("rte").h, ri - 3)
                    bkr = banks[2 + ri % 4]
                    for ci, (xs, ws) in enumerate(((xnf_hi, wr_hi),
                                                   (xnf_hi, wr_lo),
                                                   (xnf_lo, wr_hi))):
                        pr = bkr[:, ci * 8:ci * 8 + 8]
                        for dc in range(DC):
                            ins = t.matmul(pr,
                                           xs[:, dc, i * 128:(i + 1) * 128],
                                           ws[:, dc, :],
                                           start=(dc == 0),
                                           stop=(dc == DC - 1))
                    S("rt").inc(ins)

        @blk.vector
        def _(v):
            for g in range(8):
                H, tl = g // 4, g % 4
                v.wait_ge(S("xld").h, 16 * (g + 1))
                v.wait_ge(S("xsq").h, g + 1)
                v.reduce_sum(rt_s[:, 0:4, 2:3], x_tile[g % 2][:, :, :],
                             axis=AX.X)
                S("xred").inc(v.reduce_sum(rt_s[:, 0:4, 3:4], xsq_t[:, :, :],
                                           axis=AX.X))
                v.tensor_scalar_mul(rt_s[:, 0:4, 4:5], rt_s[:, 0:4, 2:3],
                                    1.0 / D)
                v.tensor_mul(rt_s[:, 0:4, 5:6], rt_s[:, 0:4, 4:5],
                             rt_s[:, 0:4, 4:5])
                v.tensor_scalar_mul(rt_s[:, 0:4, 6:7], rt_s[:, 0:4, 3:4],
                                    1.0 / D)
                v.tensor_sub(rt_s[:, 0:4, 6:7], rt_s[:, 0:4, 6:7],
                             rt_s[:, 0:4, 5:6])
                v.tensor_scalar_add(rt_s[:, 0:4, 0:1],
                                    rt_s[:, 0:4, 6:7], EPS)
                S("xst").inc(v.engine_nop())
                v.wait_ge(S("xsqt").h, g + 1)
                v.reciprocal(rt_s[:, 0:4, 7:8], rt_s[:, 0:4, 1:2])
                vfence(v)
                if g >= 1:
                    v.wait_ge(S("xtr").h, 8 * g)
                for i in range(4):
                    v.tensor_scalar(xnf_t[:, i, :], x_tile[g % 2][:, i, :],
                                    rt_s[:, i, 4:5], rt_s[:, i, 7:8],
                                    ALU.subtract, ALU.mult)
                    v.tensor_scalar(xn_tm[H][:, tl * 4 + i, :],
                                    x_tile[g % 2][:, i, :],
                                    rt_s[:, i, 4:5], rt_s[:, i, 7:8],
                                    ALU.subtract, ALU.mult)
                S("xfree").inc(v.engine_nop())
                S("xn").inc(v.engine_nop())
                if g == 0:
                    v.wait_ge(S("dzero").h, 96)
                for dc in range(DC):
                    tg = g * 8 + dc
                    v.wait_ge(S("xtr").h, tg + 1)
                    bk = banks[tg % 2]
                    v.tensor_copy(xnf_hi[:, dc, :], bk[:, :])
                    vfence(v)
                    v.tensor_tensor(xnf_lo[:, dc, :], bk[:, :],
                                    xnf_hi[:, dc, :], ALU.subtract)
                    if H == 0:
                        v.tensor_copy(
                            xn_fm[:, dc, tl * 512:(tl + 1) * 512], bk[:, :])
                    S("xtre").inc(v.engine_nop())
                if g >= 1:
                    gp = g - 1
                    Hp, tlp = gp // 4, gp % 4
                    for i in range(4):
                        ri = gp * 4 + i
                        v.wait_ge(S("rt").h, ri + 1)
                        bkr = banks[2 + ri % 4]
                        v.tensor_copy(rt3[:, :], bkr[:, 0:8])
                        vfence(v)
                        v.tensor_tensor(rt3[:, :], rt3[:, :], bkr[:, 8:16],
                                        ALU.add)
                        vfence(v)
                        v.tensor_tensor(
                            logits[:, Hp, tlp * 4 + i, :], rt3[:, :],
                            bkr[:, 16:24], ALU.add)
                        S("rte").inc(v.engine_nop())
            for i in range(4):
                ri = 7 * 4 + i
                v.wait_ge(S("rt").h, ri + 1)
                bkr = banks[2 + ri % 4]
                v.tensor_copy(rt3[:, :], bkr[:, 0:8])
                vfence(v)
                v.tensor_tensor(rt3[:, :], rt3[:, :], bkr[:, 8:16], ALU.add)
                vfence(v)
                v.tensor_tensor(
                    logits[:, 1, 3 * 4 + i, :], rt3[:, :], bkr[:, 16:24],
                    ALU.add)
                S("rte").inc(v.engine_nop())
            if debug:
                S("dbgb1").inc(v.engine_nop())

        if debug:
            @blk.gpsimd
            def _(g):
                g.wait_ge(S("dbgb1").h, 1)
                dh = S("dbghl")
                dh.inc(g.dma_start(dbg_hl.ap()[:, 0, :, :],
                                   xnf_hi[:, :, :]), 16)
                dh.inc(g.dma_start(dbg_hl.ap()[:, 1, :, :],
                                   xnf_lo[:, :, :]), 16)
                g.wait_ge(dh.h, dh.n)

    if upto < 2:
        mybir.codegen_inst_isa_subclasses(nc)
        return nc
    with nc.Block() as blk:
        @blk.gpsimd
        def _(g):
            S("kld").inc(g.dma_start(
                kT_sb[:, :, :],
                kT_in.ap().rearrange("(dc p) n -> p dc n", p=128)), 16)

        @blk.scalar
        def _(sc):
            sc.wait_ge(S("sm0").h, 1)
            sc.activation(scrB[:, :, :], scrA[:, :, :], AF.Exp)
            S("sm1").inc(sc.copy(rt3[0:1, 0:1], rt3[0:1, 1:2]))
            sc.wait_ge(S("sm2").h, 1)
            sc.activation(rt_s[:, :, 7:8], rt_s[:, :, 6:7], AF.Exp)
            S("sm3").inc(sc.copy(rt3[0:1, 0:1], rt3[0:1, 1:2]))

        @blk.vector
        def _(v):
            for H in range(2):
                lg = logits[:, H, :, :]
                v1 = rt_s[:, :, 0:1]
                v.reduce_max(v1, lg, axis=AX.X)
                v.tensor_tensor(scrA[:, :, :], lg,
                                v1.broadcast_to([128, NCHH, E]), ALU.is_equal)
                vfence(v)
                v.tensor_tensor(scrA[:, :, :], scrA[:, :, :],
                                iota8m.ap().broadcast_to([128, NCHH, E]),
                                ALU.mult)
                vfence(v)
                v.tensor_scalar_add(scrA[:, :, :], scrA[:, :, :], 999.0)
                vfence(v)
                v.tensor_reduce(rt_s[:, :, 1:2], scrA[:, :, :], axis=AX.X,
                                op=ALU.min)
                vfence(v)
                v.tensor_tensor(scrA[:, :, :],
                                iota8.ap().broadcast_to([128, NCHH, E]),
                                rt_s[:, :, 1:2].broadcast_to([128, NCHH, E]),
                                ALU.is_equal)
                vfence(v)
                v.scalar_tensor_tensor(scrB[:, :, :], scrA[:, :, :], -1e30, lg,
                                       ALU.mult, ALU.add)
                vfence(v)
                v.reduce_max(rt_s[:, :, 2:3], scrB[:, :, :], axis=AX.X)
                vfence(v)
                v.tensor_tensor(scrA[:, :, :], scrB[:, :, :],
                                rt_s[:, :, 2:3].broadcast_to([128, NCHH, E]),
                                ALU.is_equal)
                vfence(v)
                v.tensor_tensor(scrA[:, :, :], scrA[:, :, :],
                                iota8m.ap().broadcast_to([128, NCHH, E]),
                                ALU.mult)
                vfence(v)
                v.tensor_scalar_add(scrA[:, :, :], scrA[:, :, :], 999.0)
                vfence(v)
                v.tensor_reduce(rt_s[:, :, 3:4], scrA[:, :, :], axis=AX.X,
                                op=ALU.min)
                vfence(v)
                v.tensor_copy(eouts[:, H, 0, :], rt_s[:, :, 1])
                v.tensor_copy(eouts[:, H, 1, :], rt_s[:, :, 3])
                if H == 0:
                    v.tensor_tensor(scrA[:, :, :], lg,
                                    v1.broadcast_to([128, NCHH, E]),
                                    ALU.subtract)
                    S("sm0").inc(v.engine_nop())
                    v.wait_ge(S("sm1").h, 1)
                    v.reduce_sum(rt_s[:, :, 4:5], scrB[:, :, :], axis=AX.X)
                    v.reciprocal(rt_s[:, :, 5:6], rt_s[:, :, 4:5])
                    vfence(v)
                    v.tensor_copy(p_own[:, 0, :], rt_s[:, :, 5])
                    v.tensor_sub(rt_s[:, :, 6:7], rt_s[:, :, 2:3],
                                 rt_s[:, :, 0:1])
                    S("sm2").inc(v.engine_nop())
                    v.wait_ge(S("sm3").h, 1)
                    vfence(v)
                    v.tensor_mul(p_own[:, 1, :], rt_s[:, :, 7], rt_s[:, :, 5])
            S("top2").inc(v.engine_nop())

        @blk.tensor
        def _(t):
            t.wait_ge(S("top2").h, 1)
            n = 0
            for H in range(2):
                for k in range(2):
                    if n >= 1:
                        t.wait_ge(S("eTe").h, n)
                    S("eTr").inc(t.transpose(banks[3][0:16, 0:128],
                                             eouts[:, H, k, :], ident_f[:, :]))
                    n += 1

        @blk.vector
        def _(v):
            v.wait_ge(S("dzero").h, 96)
            n = 0
            for H in range(2):
                for k in range(2):
                    v.wait_ge(S("eTr").h, n + 1)
                    dst = eT[:, H, :].rearrange("p (f two) -> p f two",
                                                two=2)[:, :, k]
                    v.tensor_copy(dst, banks[3][0:16, 0:128])
                    S("eTe").inc(v.engine_nop())
                    n += 1
            v.tensor_scalar_mul(p2s[0:16, 0, :], eT[:, 0, :], hinv[0:16, 0:1])
            vfence(v)
            v.scalar_tensor_tensor(eAB[:, 0, :], eT[:, 1, :], hmask[0:16, 0:1],
                                   p2s[0:16, 0, :], ALU.mult, ALU.add)
            vfence(v)
            v.tensor_scalar_mul(p2s[0:16, 0, :], eT[:, 1, :], hinv[0:16, 0:1])
            vfence(v)
            v.scalar_tensor_tensor(eAB[:, 1, :], eT[:, 0, :], hmask[0:16, 0:1],
                                   p2s[0:16, 0, :], ALU.mult, ALU.add)
            vfence(v)
            for Gi in range(2):
                eG = eAB[:, Gi, :]
                for e in range(E):
                    v.tensor_single_scalar(msk8[:, e, :], eG, float(e),
                                           ALU.is_equal)
                    vfence(v)
                    v.tensor_tensor_scan(incl8[:, e, :], msk8[:, e, :],
                                         msk8[:, e, :], 0.0, ALU.add,
                                         ALU.bypass)
                    vfence(v)
                if Gi == 0:
                    v.tensor_reduce(
                        tot_cat[0:16, :].rearrange("p e -> p e ()"),
                        msk8[:, :, :], axis=AX.X, op=ALU.add)
                    S("tot0").inc(v.engine_nop())
                else:
                    v.tensor_reduce(
                        totG[:, :, :], msk8[:, :, :], axis=AX.X, op=ALU.add)
                    S("tot1").inc(v.engine_nop())
                vfence(v)
                v.tensor_sub(incl8[:, :, :], incl8[:, :, :], msk8[:, :, :])
                v.wait_ge(S(f"off{Gi}").h, 1)
                v.tensor_copy(offG[:, Gi, :], banks[3][0:16, 0:8])
                vfence(v)
                pg = posAB[:, Gi, :]
                for e in range(E):
                    v.tensor_scalar(p2s[0:16, 1, :], incl8[:, e, :],
                                    offG[:, Gi, e:e + 1], None, ALU.add)
                    vfence(v)
                    v.tensor_mul(p2s[0:16, 1, :], p2s[0:16, 1, :],
                                 msk8[:, e, :])
                    vfence(v)
                    if e == 0:
                        v.tensor_copy(pg, p2s[0:16, 1, :])
                    else:
                        v.tensor_add(pg, pg, p2s[0:16, 1, :])
                S(f"pos{Gi}").inc(v.engine_nop())
            v.tensor_scalar_mul(p2s[0:16, 2, :], posAB[:, 0, :],
                                hinv[0:16, 0:1])
            vfence(v)
            v.scalar_tensor_tensor(p2s[0:16, 4, :], posAB[:, 1, :],
                                   hmask[0:16, 0:1], p2s[0:16, 2, :],
                                   ALU.mult, ALU.add)
            vfence(v)
            v.tensor_scalar_mul(p2s[0:16, 2, :], posAB[:, 1, :],
                                hinv[0:16, 0:1])
            vfence(v)
            v.scalar_tensor_tensor(p2s[0:16, 5, :], posAB[:, 0, :],
                                   hmask[0:16, 0:1], p2s[0:16, 2, :],
                                   ALU.mult, ALU.add)
            vfence(v)
            for which, pos_c in ((0, 4), (1, 5)):
                pos = p2s[0:16, pos_c, :]
                est = eT[:, which, :]
                v.scalar_tensor_tensor(p2s[0:16, 6, :], est, float(CAP), pos,
                                       ALU.mult, ALU.add)
                v.tensor_single_scalar(p2s[0:16, 7, :], pos, float(CAP),
                                       ALU.is_lt)
                vfence(v)
                v.tensor_scalar_add(p2s[0:16, 3, :], p2s[0:16, 6, :],
                                    -float(DUMP))
                vfence(v)
                v.tensor_mul(p2s[0:16, 3, :], p2s[0:16, 3, :], p2s[0:16, 7, :])
                vfence(v)
                v.tensor_scalar_add(p2s[0:16, 3, :], p2s[0:16, 3, :],
                                    float(DUMP))
                vfence(v)
                v.tensor_copy(eAB[:, which, :], p2s[0:16, 3, :])
                if which == 0:
                    v.tensor_mul(posAB[:, 0, :], p2s[0:16, 6, :],
                                 p2s[0:16, 7, :])
                vfence(v)
            S("lists").inc(v.engine_nop())

        @blk.sync
        def _(sy):
            sy.wait_ge(S("tot1").h, 1)
            S("tcat").inc(sy.dma_start(tot_cat[16:32, :], totG[:, :, 0]), 16)

        @blk.tensor
        def _(t):
            t.wait_ge(S("tot0").h, 1)
            S("off0").inc(t.matmul(banks[3][0:16, 0:8], tril16[:, :],
                                   tot_cat[0:16, :], start=True, stop=True))
            t.wait_ge(S("tcat").h, 16)
            t.wait_ge(S("pos0").h, 1)
            S("off1").inc(t.matmul(banks[3][0:16, 0:8], m32b[:, :],
                                   tot_cat[:, :], start=True, stop=True))
            nl = 0
            for li in range(6):
                t.wait_ge(S("wcp").h, nl + 1)
                for sp in range(8):
                    n = nl * 8 + sp
                    if n >= 2:
                        t.wait_ge(S("wre").h, n - 1)
                    S("wr").inc(t.transpose(
                        banks[4 + n % 2][0:16, 0:16],
                        wtmp[:, nl % 2, sp * 16:(sp + 1) * 16],
                        ident_f[0:16, 0:16]))
                nl += 1
            for k in range(2):
                t.wait_ge(S("wbcp").h, k + 1)
                S("wb").inc(t.transpose(banks[6 + k][:, 0:16],
                                        wtmp[:, k, :],
                                        ident_f[0:16, 0:16]))

        @blk.vector
        def _(v):
            arrs = [(eAB[:, 0, :], 0), (eAB[:, 0, :], 1),
                    (eAB[:, 1, :], 0), (eAB[:, 1, :], 1),
                    (posAB[:, 0, :], 0), (posAB[:, 0, :], 1)]
            n = 0
            for nl, (arr, k) in enumerate(arrs):
                if nl >= 2:
                    v.wait_ge(S("wr").h, (nl - 1) * 8)
                v.tensor_copy(
                    wtmp[:, nl % 2, :],
                    arr.rearrange("p (q k) -> p q k", k=2)[:, :, k])
                S("wcp").inc(v.engine_nop())
                for sp in range(8):
                    v.wait_ge(S("wr").h, n + 1)
                    dst = idx_all[0:16, nl, :].rearrange(
                        "p (q sp) -> p q sp", sp=8)[:, :, sp]
                    v.tensor_copy(dst, banks[4 + n % 2][0:16, 0:16])
                    S("wre").inc(v.engine_nop())
                    n += 1
            for k in range(2):
                v.wait_ge(S("wr").h, 48)
                v.tensor_copy(
                    wtmp[:, k, :],
                    p2s[0:16, 4, :].rearrange("p (q k) -> p q k", k=2)[:, :, k])
                S("wbcp").inc(v.engine_nop())
                v.wait_ge(S("wb").h, k + 1)
                v.tensor_copy(rt_s[:, :, 8 + k], banks[6 + k][:, 0:16])
                S("wbe").inc(v.engine_nop())
            for k in range(2):
                v.tensor_single_scalar(rt_s[:, :, 8 + k], rt_s[:, :, 8 + k],
                                       float(CAP), ALU.is_lt)
                vfence(v)
                v.tensor_mul(wgt[:, k, :], p_own[:, k, :], rt_s[:, :, 8 + k])
            S("wgtd").inc(v.engine_nop())

        @blk.sync
        def _(sy):
            sy.wait_ge(S("wre").h, 48)
            S("irep").inc(sy.dma_start(idx_all[16:32, :, :],
                                       idx_all[0:16, :, :]), 16)
            if debug:
                sy.wait_ge(S("wgtd").h, 1)
                db2 = S("dbgb2")
                db2.inc(sy.dma_start(dbg_rt.ap()[:, :, :], rt_s[:, :, :]), 16)
                db2.inc(sy.dma_start(dbg_p.ap()[:, :, :], p_own[:, :, :]), 16)
                db2.inc(sy.dma_start(dbg_eouts.ap()[:, :, :, :],
                                     eouts[:, :, :, :]), 16)
                db2.inc(sy.dma_start(dbg_eab.ap()[:, :, :], eAB[:, :, :]), 16)
                db2.inc(sy.dma_start(dbg_pos.ap()[:, :, :],
                                     posAB[:, :, :]), 16)
                db2.inc(sy.dma_start(dbg_p2s.ap()[:, :, :], p2s[:, :, :]), 16)
                db2.inc(sy.dma_start(dbg_off.ap()[0:16, 0:2, :],
                                     offG[:, :, :]), 16)
                db2.inc(sy.dma_start(dbg_eT.ap()[:, :, :], eT[:, :, :]), 16)
                db2.inc(sy.dma_start(dbg_off.ap()[0:32, 2, :],
                                     tot_cat[:, :]), 16)
                sy.wait_ge(db2.h, db2.n)

    if upto < 3:
        mybir.codegen_inst_isa_subclasses(nc)
        return nc
    NEXP = E if upto >= 4 else 0
    with nc.Block() as blk:
        @blk.gpsimd
        def _(g):
            g.wait_ge(S("dzero").h, 96)
            g.wait_ge(S("irep").h, 16)
            for li, src in ((0, xn_tm[0]), (1, xn_tm[0]),
                            (2, xn_tm[1]), (3, xn_tm[1])):
                S("scat").inc(g.dma_scatter_add(
                    disp.ap()[:, :], src[:, :, :], idx_all[:, li, :],
                    TOWN, TOWN, D), 16)
            g.wait_ge(S("scat").h, 64)
            S("vld").inc(g.dma_start(
                v_sb[:, :, :],
                v_in.ap().rearrange("(nc p) d -> p nc d", p=128)), 16)
            for e in range(NEXP):
                if e == 0:
                    g.wait_ge(S("sdone").h, NTT)
                else:
                    g.wait_ge(S("w1done").h, e)
                S("w1l").inc(g.dma_start(
                    w1_sb[:, :, :],
                    w1_in.ap()[e].rearrange("(dc p) h -> p dc h", p=128)), 16)
                if e >= 1:
                    g.wait_ge(S("w2c").h, e)
                S("w2l").inc(g.dma_start(
                    w2_sb[:, :, :],
                    w2_in.ap()[e].rearrange("(hc p) c -> p hc c", p=128)), 16)
                if e >= 1:
                    g.wait_ge(S("b2c").h, e)
                S("b2l").inc(g.dma_start(
                    b2b_sb[:, :],
                    b2_in.ap()[e:e + 1, :].broadcast_to([128, D])), 16)
            if upto >= 5:
                g.wait_ge(S("yh").h, 16 * 5 * E)
                for dst, li in ((gA, 4), (gB, 5)):
                    for ci in range(2):
                        S("gath").inc(g.dma_gather(
                            dst[:, ci * 8:(ci + 1) * 8, :], y_hbm.ap()[:, :],
                            idx_all[:, li, ci * 64:(ci + 1) * 64],
                            1024, 1024, D), 16)

        @blk.sync
        def _(sy):
            for tt in range(NTT if attn_sub >= 3 else 0):
                sy.wait_ge(S("sev").h, tt * 4 + 4)
                dst = s_hbm.ap().rearrange("(c p) d -> p c d", p=128)
                S("sh").inc(sy.dma_start(dst[:, tt * 2:tt * 2 + 2, :],
                                         s_tile[tt % 2][:, :, :]), 16)
            for e in range(NEXP):
                if e == 0:
                    sy.wait_ge(S("scat").h, 64)
                    sy.wait_ge(S("sdone").h, NTT)
                    sy.wait_ge(S("sh").h, 16 * NTT)
                else:
                    sy.wait_ge(S("btr").h, e)
                src = disp.ap()[e * CAP:(e + 1) * CAP, :].rearrange(
                    "(g p) d -> p g d", p=128)
                S("bufl").inc(sy.dma_start(buf_tm[:, :, :], src), 16)
                if e >= 1:
                    for sc_i in range(5):
                        ep = e - 1
                        sy.wait_ge(S("ye").h, ep * 10 + sc_i * 2 + 2)
                        dst = y_hbm.ap()[ep * CAP + sc_i * 128:
                                         ep * CAP + (sc_i + 1) * 128, :]
                        S("yh").inc(sy.dma_start(dst,
                                                 y_tile[sc_i % 2][:, :]), 16)
            if upto >= 4:
                for sc_i in range(5):
                    sy.wait_ge(S("ye").h, 7 * 10 + sc_i * 2 + 2)
                    dst = y_hbm.ap()[7 * CAP + sc_i * 128:
                                     7 * CAP + (sc_i + 1) * 128, :]
                    S("yh").inc(sy.dma_start(dst, y_tile[sc_i % 2][:, :]), 16)
            if upto >= 5:
                sy.wait_ge(S("w2c").h, E)
                S("xrl").inc(sy.dma_start(
                    x_own_sb[:, :, :],
                    x_own.ap().rearrange("(c p) d -> p c d", p=128)), 16)
                sy.wait_ge(S("hfree").h, 1)
                S("srl").inc(sy.dma_start(
                    s_sb[:, :, :],
                    s_hbm.ap().rearrange("(c p) d -> p c d", p=128)), 16)
                for c in range(NCHH):
                    sy.wait_ge(S("oev").h, c + 1)
                    dst = out_t.ap().rearrange("(c p) d -> p c d",
                                               p=128)[:, c, :]
                    S("ost").inc(sy.dma_start(dst, o_tile[c % 2][:, :]), 16)
                sy.wait_ge(S("ost").h, 16 * NCHH)
            elif upto == 4:
                sy.wait_ge(S("yh").h, 16 * 5 * E)
            else:
                if attn_sub >= 3:
                    sy.wait_ge(S("sh").h, 16 * NTT)
                elif attn_sub >= 1:
                    sy.wait_ge(S("awe").h if attn_sub == 1 else S("nrm").h,
                               NTT * NC_N if attn_sub == 1 else NTT)
                else:
                    sy.wait_ge(S("vld").h, 16)

        @blk.scalar
        def _(sc):
            sc.wait_ge(S("dzero").h, 96)
            for tt in range(NTT if attn_sub >= 2 else 0):
                for ncg in range(NC_N):
                    j = tt * NC_N + ncg
                    sc.wait_ge(S("awe").h, j + 1)
                    if j >= 2:
                        sc.wait_ge(S("acc").h, j - 1)
                    sc.activation(sq_sb[:, ncg % 2, :],
                                  aw_sb[:, ncg, :], AF.Square)
                    S("sq2").inc(sc.copy(scrA[0:1, 0, 0:1], scrA[0:1, 0, 1:2]))
                sc.wait_ge(S("l2d").h, tt + 1)
                sc.activation(
                    c_f[:, :],
                    banks[4][:, (tt % 2) * TT:(tt % 2) * TT + TT], AF.Sqrt)
                S("cs").inc(sc.copy(scrA[0:1, 0, 0:1], scrA[0:1, 0, 1:2]))
                sc.wait_ge(S("nrm").h, tt + 1)
                sc.activation(aw_sb[:, :, :], aw_sb[:, :, :], AF.Gelu)
                S("gw").inc(sc.copy(scrA[0:1, 0, 0:1], scrA[0:1, 0, 1:2]))

            for c in range(NCHH if upto >= 5 else 0):
                sc.wait_ge(S("fev").h, c + 1)
                if c >= 1:
                    sc.wait_ge(S("oev").h, c)
                sc.activation(g_tile[:, :], f_tile[:, :], AF.Silu)
                S("gsil").inc(sc.copy(scrA[0:1, 0, 0:1], scrA[0:1, 0, 1:2]))

        @blk.tensor
        def _(t):
            t.wait_ge(S("kld").h, 16)
            for tt in range(NTT if attn_sub >= 1 else 0):
                for ncg in range(NC_N):
                    j = tt * NC_N + ncg
                    bk = banks[j % 4][:, ((j // 4) % 2) * TT:
                                      ((j // 4) % 2) * TT + TT]
                    if j >= 4:
                        t.wait_ge(S("awe").h, j - 3)
                    for dc in range(DC):
                        ins = t.matmul(bk,
                                       kT_sb[:, dc, ncg * 128:(ncg + 1) * 128],
                                       xn_fm[:, dc, tt * TT:(tt + 1) * TT],
                                       start=(dc == 0), stop=(dc == DC - 1))
                    S("aw").inc(ins)
                if attn_sub < 2:
                    continue
                if tt >= 1:
                    t.wait_ge(S("cs").h, tt)
                t.wait_ge(S("acc").h, (tt + 1) * NC_N)
                l2bk = banks[4][:, (tt % 2) * TT:(tt % 2) * TT + TT]
                ins = t.matmul(l2bk, ones_nn[:, :], sqacc_b[:, :],
                               start=True, stop=True)
                S("l2d").inc(ins)
                if attn_sub < 3:
                    continue
                t.wait_ge(S("gw").h, tt + 1)
                if tt == 0:
                    t.wait_ge(S("vld").h, 16)
                for ts_ in range(2):
                    for dh in range(2):
                        q = tt * 4 + ts_ * 2 + dh
                        bk = banks[6 + dh]
                        if q >= 2:
                            t.wait_ge(S("sev").h, q - 1)
                        for ncg in range(NC_N):
                            ins = t.matmul(
                                bk[:, :],
                                aw_sb[:, ncg, ts_ * 128:(ts_ + 1) * 128],
                                v_sb[:, ncg, dh * 512:(dh + 1) * 512],
                                start=(ncg == 0), stop=(ncg == NC_N - 1))
                        S("s").inc(ins)
                S("sdone").inc(t.matmul(banks[5][:, 0:1], ident_b[:, :],
                                        ones_n[:, :],
                                        start=True, stop=True))
            for e in range(NEXP):
                t.wait_ge(S("bufl").h, 16 * (e + 1))
                if e >= 1:
                    t.wait_ge(S("w1done").h, e)
                for jj in range(40):
                    gsl, dc = jj // DC, jj % DC
                    jg = e * 40 + jj
                    if jg >= 2:
                        t.wait_ge(S("btre").h, jg - 1)
                    ins = t.transpose(banks_b[jg % 2][:, 0:128],
                                      buf_tm[:, gsl, dc * 128:(dc + 1) * 128],
                                      ident_b[:, :])
                    S("btr2").inc(ins)
                S("btr").inc(t.matmul(banks[2][:, 0:1], ident_b[:, :],
                                      ones_n[:, :],
                                      start=True, stop=True))
                t.wait_ge(S("w1l").h, 16 * (e + 1))
                t.wait_ge(S("btre").h, (e + 1) * 40)
                for hc in range(NC_H):
                    jh = e * NC_H + hc
                    bkA = banks[2 + (hc % 2)]
                    bkB = banks[4 + (hc % 2)]
                    if jh >= 2:
                        t.wait_ge(S("he").h, jh - 1)
                    for dc in range(DC):
                        lhs = w1_sb[:, dc, hc * 128:(hc + 1) * 128]
                        t.matmul(bkA[:, :], lhs, buf_fm[:, dc, 0:512],
                                 start=(dc == 0), stop=(dc == DC - 1))
                        ins = t.matmul(bkB[:, 0:128], lhs,
                                       buf_fm[:, dc, 512:640],
                                       start=(dc == 0), stop=(dc == DC - 1))
                    S("h").inc(ins)
                S("w1done").inc(t.matmul(banks[6][:, 0:1], ident_b[:, :],
                                         ones_n[:, :],
                                         start=True, stop=True))
                t.wait_ge(S("w2l").h, 16 * (e + 1))
                t.wait_ge(S("he").h, (e + 1) * NC_H)
                for sc_i in range(5):
                    for ch in range(2):
                        q = e * 10 + sc_i * 2 + ch
                        bk = banks[6 + (q % 2)]
                        if q >= 2:
                            t.wait_ge(S("ye").h, q - 1)
                        for hc in range(NC_H):
                            hsrc = h_lo if hc < 24 else h_hi
                            hcc = hc if hc < 24 else hc - 24
                            ins = t.matmul(
                                bk[:, :],
                                hsrc[:, hcc, sc_i * 128:(sc_i + 1) * 128],
                                w2_sb[:, hc, ch * 512:(ch + 1) * 512],
                                start=(hc == 0), stop=(hc == NC_H - 1))
                        S("y").inc(ins)
                S("w2c").inc(t.matmul(banks[0][:, 0:1], ident_b[:, :],
                                      ones_n[:, :],
                                      start=True, stop=True))

        @blk.vector
        def _(v):
            v.wait_ge(S("dzero").h, 96)
            for tt in range(NTT if attn_sub >= 1 else 0):
                for ncg in range(NC_N):
                    j = tt * NC_N + ncg
                    v.wait_ge(S("aw").h, j + 1)
                    if tt >= 1 and ncg == 0 and attn_sub >= 3:
                        v.wait_ge(S("sdone").h, tt)
                    bk = banks[j % 4][:, ((j // 4) % 2) * TT:
                                      ((j // 4) % 2) * TT + TT]
                    v.tensor_copy(aw_sb[:, ncg, :], bk)
                    S("awe").inc(v.engine_nop())
                    if attn_sub >= 2:
                        v.wait_ge(S("sq2").h, j + 1)
                        if ncg == 0:
                            v.tensor_copy(sqacc[:, :], sq_sb[:, 0, :])
                        elif ncg == NC_N - 1:
                            v.tensor_tensor(sqacc[:, :], sqacc[:, :],
                                            sq_sb[:, ncg % 2, :], ALU.add)
                            if tt >= 1:
                                v.wait_ge(S("l2d").h, tt)
                            vfence(v)
                            v.tensor_copy(sqacc_b[:, :], sqacc[:, :])
                        else:
                            v.tensor_tensor(sqacc[:, :], sqacc[:, :],
                                            sq_sb[:, ncg % 2, :], ALU.add)
                        S("acc").inc(v.engine_nop())
                if attn_sub < 2:
                    continue
                v.wait_ge(S("cs").h, tt + 1)
                v.reciprocal(c_f[:, :], c_f[:, :])
                vfence(v)
                v.tensor_scalar_mul(c_b[:, 0, :], c_f[:, :], 64.0)
                v.tensor_mul(aw_sb[:, :, :], aw_sb[:, :, :],
                             c_b.ap().broadcast_to([128, NC_N, TT]))
                S("nrm").inc(v.engine_nop())
                if attn_sub < 3:
                    continue
                for ts_ in range(2):
                    for dh in range(2):
                        q = tt * 4 + ts_ * 2 + dh
                        v.wait_ge(S("s").h, q + 1)
                        if tt >= 2 and ts_ == 0 and dh == 0:
                            v.wait_ge(S("sh").h, 16 * (tt - 1))
                        v.tensor_copy(
                            s_tile[tt % 2][:, ts_, dh * 512:(dh + 1) * 512],
                            banks[6 + dh][:, :])
                        S("sev").inc(v.engine_nop())
            for e in range(NEXP):
                for jj in range(40):
                    jg = e * 40 + jj
                    v.wait_ge(S("btr2").h, jg + 1)
                    gsl, dc = jj // DC, jj % DC
                    v.tensor_copy(
                        buf_fm[:, dc, gsl * 128:(gsl + 1) * 128],
                        banks_b[jg % 2][:, 0:128])
                    S("btre").inc(v.engine_nop())
                for hc in range(NC_H):
                    v.wait_ge(S("h").h, e * NC_H + hc + 1)
                    bkA = banks[2 + (hc % 2)]
                    bkB = banks[4 + (hc % 2)]
                    hdst = h_lo if hc < 24 else h_hi
                    hcc = hc if hc < 24 else hc - 24
                    v.tensor_scalar(hdst[:, hcc, 0:512], bkA[:, :],
                                    b1_sb[:, e, hc:hc + 1], 0.0,
                                    ALU.add, ALU.max)
                    v.tensor_scalar(hdst[:, hcc, 512:640], bkB[:, 0:128],
                                    b1_sb[:, e, hc:hc + 1], 0.0,
                                    ALU.add, ALU.max)
                    S("he").inc(v.engine_nop())
                v.wait_ge(S("b2l").h, 16 * (e + 1))
                for sc_i in range(5):
                    for ch in range(2):
                        q = e * 10 + sc_i * 2 + ch
                        v.wait_ge(S("y").h, q + 1)
                        si = e * 5 + sc_i
                        if si >= 2:
                            v.wait_ge(S("yh").h, 16 * (si - 1))
                        bk = banks[6 + (q % 2)]
                        v.tensor_tensor(
                            y_tile[sc_i % 2][:, ch * 512:(ch + 1) * 512],
                            bk[:, :], b2b_sb[:, ch * 512:(ch + 1) * 512],
                            ALU.add)
                        S("ye").inc(v.engine_nop())
                S("b2c").inc(v.engine_nop())
            S("hfree").inc(v.engine_nop())
            if upto < 5:
                return
            v.wait_ge(S("gath").h, 64)
            v.wait_ge(S("xrl").h, 16)
            v.wait_ge(S("srl").h, 16)
            v.wait_ge(S("wgtd").h, 1)
            for c in range(NCHH):
                if c >= 2:
                    v.wait_ge(S("ost").h, 16 * (c - 1))
                v.tensor_scalar_mul(f_tile[:, :], gB[:, c, :],
                                    wgt[:, 1, c:c + 1])
                vfence(v)
                v.scalar_tensor_tensor(
                    f_tile[:, :], gA[:, c, :], wgt[:, 0, c:c + 1],
                    f_tile[:, :], ALU.mult, ALU.add)
                S("fev").inc(v.engine_nop())
                v.wait_ge(S("gsil").h, c + 1)
                v.tensor_mul(g_tile[:, :], g_tile[:, :], s_sb[:, c, :])
                vfence(v)
                v.tensor_add(o_tile[c % 2][:, :], g_tile[:, :],
                             x_own_sb[:, c, :])
                S("oev").inc(v.engine_nop())
            if debug:
                S("dbg").inc(v.engine_nop())

        if debug:
            @blk.sync
            def _(sy):
                sy.wait_ge(S("dbg").h, 1)
                dd = S("dbgd")
                dd.inc(sy.dma_start(dbg_logits.ap()[:, :, :, :],
                                    logits[:, :, :, :]), 16)
                dd.inc(sy.dma_start(dbg_wgt.ap()[:, :, :], wgt[:, :, :]), 16)
                dd.inc(sy.dma_start(dbg_idx.ap()[:, :, :],
                                    idx_all[:, :, :]), 16)
                sy.wait_ge(dd.h, dd.n)

            @blk.gpsimd
            def _(g):
                g.wait_ge(S("dbg").h, 1)
                dd2 = S("dbg2")
                dd2.inc(g.dma_start(dbg_disp.ap()[:, :], disp.ap()[:, :]), 16)
                dd2.inc(g.dma_start(dbg_s.ap()[:, :], s_hbm.ap()[:, :]), 16)
                dd2.inc(g.dma_start(dbg_y.ap()[:, :], y_hbm.ap()[:, :]), 16)
                dd2.inc(g.dma_start(dbg_g.ap()[:, 0, :, :], gA[:, :, :]), 16)
                dd2.inc(g.dma_start(dbg_g.ap()[:, 1, :, :], gB[:, :, :]), 16)
                g.wait_ge(dd2.h, dd2.n)

    mybir.codegen_inst_isa_subclasses(nc)
    return nc


_CACHE = {}


def kernel(L0_attn_inputs, L0_attn_key_param_tokens, L0_attn_value_param_tokens,
           L0_ffn_Wr, L0_ffn_W1, L0_ffn_b1, L0_ffn_W2, L0_ffn_b2):
    if "nc" not in _CACHE:
        _CACHE["nc"] = build_program()
    nc = _CACHE["nc"]
    x = np.ascontiguousarray(np.asarray(L0_attn_inputs, dtype=np.float32))
    kT = np.ascontiguousarray(np.asarray(L0_attn_key_param_tokens,
                                         dtype=np.float32).T)
    v = np.ascontiguousarray(np.asarray(L0_attn_value_param_tokens,
                                        dtype=np.float32))
    wr = np.ascontiguousarray(np.asarray(L0_ffn_Wr, dtype=np.float32)
                              .reshape(DC, 128, E).transpose(1, 0, 2))
    w1 = np.ascontiguousarray(np.asarray(L0_ffn_W1, dtype=np.float32))
    b1 = np.ascontiguousarray(np.asarray(L0_ffn_b1, dtype=np.float32)
                              .reshape(E, NC_H, 128).transpose(2, 0, 1))
    w2 = np.ascontiguousarray(np.asarray(L0_ffn_W2, dtype=np.float32))
    b2 = np.ascontiguousarray(np.asarray(L0_ffn_b2, dtype=np.float32))

    in_maps = []
    for c in range(8):
        b, h = c // 2, c % 2
        in_maps.append({
            "x_own": np.ascontiguousarray(x[b, h * TOWN:(h + 1) * TOWN]),
            "x_oth": np.ascontiguousarray(x[b, (1 - h) * TOWN:(2 - h) * TOWN]),
            "hmask": np.full((128, 1), float(h), dtype=np.float32),
            "kT": kT, "v": v, "wr": wr, "w1": w1, "b1": b1, "w2": w2, "b2": b2,
        })
    res = run_bass_kernel_spmd(nc, in_maps, core_ids=list(range(8)))
    out = np.empty((B, TFULL, D), dtype=np.float32)
    for c in range(8):
        b, h = c // 2, c % 2
        out[b, h * TOWN:(h + 1) * TOWN] = res.results[c]["out"]
    return out
